# version 05
# baseline (speedup 1.0000x reference)
import numpy as np
import concourse.bass as bass
import concourse.mybir as mybir
from concourse.bass_utils import run_bass_kernel_spmd
import contextlib

F32 = mybir.dt.float32
BF16 = mybir.dt.bfloat16
AF = mybir.ActivationFunctionType
ALU = mybir.AluOpType
AXX = mybir.AxisListType.X

S = 2048
E = 1024
NT = S // 128
NG = E // 128
HG = 4
DFF = 4096
NEXP = 4
EPS = 1e-5

ENGS = ("tensor", "vector", "scalar", "sync", "gpsimd")
SEMS = ENGS + ("dL", "dIO")


class Res:
    __slots__ = ("w", "readers")

    def __init__(self):
        self.w = None
        self.readers = {}

    def rd(self):
        return [self.w] if self.w else []

    def wr(self):
        d = [self.w] if self.w else []
        return d + [(s, v, i) for s, (v, i) in self.readers.items()]

    def setw(self, tok):
        if tok is not None:
            self.w = tok
            self.readers = {}

    def addr(self, tok):
        if tok is not None:
            s, v, i = tok
            old = self.readers.get(s)
            if old is None or v > old[0]:
                self.readers[s] = (v, i)


def build_program(n_seq, dbg=False):
    nc = bass.Bass("TRN2", target_bir_lowering=False)

    d_x = nc.dram_tensor("x", [n_seq, S, E], F32, kind="ExternalInput").ap()
    d_y = nc.dram_tensor("y", [n_seq, S, E], F32, kind="ExternalOutput").ap()
    d_wqT = nc.dram_tensor("wqT", [E, E], F32, kind="ExternalInput").ap()
    d_wkT = nc.dram_tensor("wkT", [E, E], F32, kind="ExternalInput").ap()
    d_wvT = nc.dram_tensor("wvT", [E, E], F32, kind="ExternalInput").ap()
    d_woT = nc.dram_tensor("woT", [E, E], F32, kind="ExternalInput").ap()
    d_bq = nc.dram_tensor("bq", [128, NG], F32, kind="ExternalInput").ap()
    d_bk = nc.dram_tensor("bk", [128, NG], F32, kind="ExternalInput").ap()
    d_bvb = nc.dram_tensor("bvb", [128, E], F32, kind="ExternalInput").ap()
    d_bob = nc.dram_tensor("bob", [128, E], F32, kind="ExternalInput").ap()
    d_aw = nc.dram_tensor("aw", [E, 3 * E], F32, kind="ExternalInput").ap()
    d_pw = nc.dram_tensor("pw", [E, E], F32, kind="ExternalInput").ap()
    d_cw = nc.dram_tensor("cw", [128, NG, 3], F32, kind="ExternalInput").ap()
    d_ln1w = nc.dram_tensor("ln1w", [128, NG], F32, kind="ExternalInput").ap()
    d_ln1b = nc.dram_tensor("ln1b", [128, NG], F32, kind="ExternalInput").ap()
    d_ln2w = nc.dram_tensor("ln2w", [128, NG], F32, kind="ExternalInput").ap()
    d_ln2b = nc.dram_tensor("ln2b", [128, NG], F32, kind="ExternalInput").ap()
    d_rw = nc.dram_tensor("rw", [128, NG, NEXP], F32, kind="ExternalInput").ap()
    d_ewv = nc.dram_tensor("ewv", [NEXP, E, 2 * DFF], F32, kind="ExternalInput").ap()
    d_epr = nc.dram_tensor("epr", [NEXP, DFF, E], F32, kind="ExternalInput").ap()
    d_cos = nc.dram_tensor("costab", [128, S], F32, kind="ExternalInput").ap()
    d_sin = nc.dram_tensor("sintab", [128, S], F32, kind="ExternalInput").ap()
    d_mb = nc.dram_tensor("maskbin", [128, 896], F32, kind="ExternalInput").ap()
    d_r1 = nc.dram_tensor("r1t", [128, 128], F32, kind="ExternalInput").ap()
    d_r2 = nc.dram_tensor("r2t", [128, 128], F32, kind="ExternalInput").ap()
    d_id = nc.dram_tensor("identin", [128, 128], F32, kind="ExternalInput").ap()
    d_ones = nc.dram_tensor("onesin", [1, 128], F32, kind="ExternalInput").ap()

    cur = [16512]

    def sb(name, shape, dtype, at=None):
        nbytes = int(np.prod(shape[1:])) * (2 if dtype == BF16 else 4)
        if at is None:
            at = cur[0]
            cur[0] = (at + nbytes + 31) & ~31
        h = nc.alloc_sbuf_tensor_at(name, shape, dtype, offset=at)
        return h

    acc = sb("acc", [128, NT, E], F32)
    u = sb("u", [128, NT, E], BF16)
    r1_s = sb("r1_s", [128, 128], BF16)
    r2_s = sb("r2_s", [128, 128], BF16)
    id_s = sb("id_s", [128, 128], BF16)
    ones_s = sb("ones_s", [1, 128], BF16)
    bq_s = sb("bq_s", [128, NG], F32)
    bk_s = sb("bk_s", [128, NG], F32)
    cw_s = sb("cw_s", [128, NG, 3], F32)
    ln1w_s = sb("ln1w_s", [128, NG], F32)
    ln1b_s = sb("ln1b_s", [128, NG], F32)
    ln2w_s = sb("ln2w_s", [128, NG], F32)
    ln2b_s = sb("ln2b_s", [128, NG], F32)
    rw_s = sb("rw_s", [128, NG, NEXP], F32)
    st16 = sb("st16", [128, 4, NT], F32)
    xnf = sb("xnf", [128, E], F32)
    xnfb = sb("xnfb", [128, 2 * E], BF16, at=xnf.manual_sbuf_range[0])
    mrow = sb("mrow", [1, NEXP, S], BF16)

    A0 = cur[0]
    xnT = sb("xnT", [128, NG, S], BF16, at=A0)
    h1c = sb("h1c", [128, NG, S], BF16, at=A0)
    h2T = sb("h2T", [128, NG, S], BF16, at=A0)
    B0 = A0 + NG * S * 2 + 32

    o = B0
    def place(name, shape, dtype):
        nonlocal_o = place.o
        h = sb(name, shape, dtype, at=nonlocal_o)
        nbytes = int(np.prod(shape[1:])) * (2 if dtype == BF16 else 4)
        place.o = (nonlocal_o + nbytes + 31) & ~31
        return h
    place.o = B0
    wq_b = place("wq_b", [128, NG, 256], BF16)
    wk_b = place("wk_b", [128, NG, 256], BF16)
    wv_b = place("wv_b", [128, NG, 256], BF16)
    wo_b = place("wo_b", [128, 2, E], BF16)
    Kr0 = place("Kr0", [128, S], BF16)
    Kr1 = place("Kr1", [128, S], BF16)
    Vg = place("Vg", [128, NT, 4, 65], BF16)
    Qrs = [[place(f"Qr{g}_{j}", [128, 512], BF16) for j in range(2)] for g in range(2)]
    Ob0 = place("Ob0", [128, 512], BF16)
    Ob1 = place("Ob1", [128, 512], BF16)
    EXs = [place(f"EX{j}", [128, 512], BF16) for j in range(4)]
    tm1 = place("tm1", [128, 512], BF16)
    rb_s = sb("rb_s", [64, 512], BF16, at=xnf.manual_sbuf_range[0])
    rinv = sb("rinv", [1, 512], BF16, at=xnf.manual_sbuf_range[0] + 1024)
    cos_s = place("cos_s", [128, S], BF16)
    sin_s = place("sin_s", [128, S], BF16)
    mb_s = place("mb_s", [128, 896], BF16)
    bvb_s = place("bvb_s", [128, E], BF16)
    bob_s = place("bob_s", [128, E], BF16)
    attn_end = place.o

    place.o = B0
    n1buf = place("n1buf", [128, NT, E], BF16)
    ct = place("ct", [128, S], F32)
    ct2 = place("ct2", [128, S], F32)
    logits = place("lg", [NEXP, S], F32)
    lnt_end = place.o
    lrow = sb("lrow", [1, 3, S], F32, at=B0)
    mtmp = sb("mtmp", [1, 3, S], F32, at=B0 + 3 * S * 4 + 32)

    place.o = B0
    z_s = place("z_s", [128, 8, 1024], BF16)
    wg_b = [place(f"wg_b{j}", [128, NG, 256], BF16) for j in range(2)]
    wv2_b = [place(f"wv2_b{j}", [128, NG, 256], BF16) for j in range(2)]
    P_b = [place(f"P_b{j}", [128, 4, E], BF16) for j in range(2)]
    maskb = place("maskb", [128, 1024], BF16)
    sgs = place("sgs", [128, 512], F32)
    moe_end = place.o
    assert max(attn_end, lnt_end, moe_end) <= 229344, (attn_end, lnt_end, moe_end)

    pA = nc.alloc_psum_tensor("pA", [128, 512], F32)
    pB = nc.alloc_psum_tensor("pB", [128, 512], F32)
    pC = nc.alloc_psum_tensor("pC", [128, 512], F32)
    pD = nc.alloc_psum_tensor("pD", [128, 512], F32)
    pE = nc.alloc_psum_tensor("pE", [128, 512], F32)
    pF = nc.alloc_psum_tensor("pF", [128, 512], F32)
    pG = nc.alloc_psum_tensor("pG", [128, 512], F32)
    pT = nc.alloc_psum_tensor("pT", [128, 1024], BF16)

    ops = {e: [] for e in ENGS}
    cnt = {s: 0 for s in SEMS}
    lastwait = {e: {} for e in ENGS}
    bar_deps = []
    R = {}

    def res(n):
        if n not in R:
            R[n] = Res()
        return R[n]

    def em(eng, fn, sig=False, inc=None, reads=(), writes=(), deps=()):
        alldeps = list(bar_deps) + list(deps)
        for r in reads:
            alldeps += res(r).rd()
        for r in writes:
            alldeps += res(r).wr()
        waits = []
        for d in alldeps:
            if d is None:
                continue
            s_, v_, _ = d
            if s_ == eng:
                continue
            if v_ > lastwait[eng].get(s_, 0):
                lastwait[eng][s_] = v_
                waits.append((s_, v_))
        tok = None
        if sig:
            sem = inc_sem = eng if inc is None else None
            if inc is None:
                semn, amt = eng, 1
            else:
                semn, amt = inc
            cnt[semn] += amt
            tok = (semn, cnt[semn], amt)
        ops[eng].append((waits, fn, tok))
        for r in reads:
            res(r).addr(tok)
        for r in writes:
            res(r).setw(tok)
        return tok

    def barrier(dma_too=True):
        toks = []
        for e in ENGS:
            if ops[e] and ops[e][-1][2] is None:
                toks.append(em(e, lambda x: x.nop(), sig=True))
            elif ops[e]:
                toks.append(ops[e][-1][2])
        if dma_too:
            for dn in ("dL", "dIO"):
                if cnt[dn]:
                    toks.append((dn, cnt[dn], 16))
        bar_deps.clear()
        bar_deps.extend([t for t in toks if t is not None])

    def dma(queue_eng, semn, out_ap, in_ap, reads=(), writes=()):
        return em(queue_eng, lambda q, o=out_ap, i=in_ap: q.dma_start(o, i),
                  sig=True, inc=(semn, 16), reads=reads, writes=writes)

    for h_, dr in [(r1_s, d_r1), (r2_s, d_r2), (id_s, d_id), (ones_s, d_ones),
                   (bq_s, d_bq), (bk_s, d_bk), (cw_s, d_cw), (ln1w_s, d_ln1w),
                   (ln1b_s, d_ln1b), (ln2w_s, d_ln2w), (ln2b_s, d_ln2b), (rw_s, d_rw)]:
        dma("gpsimd", "dL", h_[:], dr)
    barrier()

    def ln_stats_all(src3, rdres):
        for ti in range(NT):
            em("vector", lambda v, o=st16[:, 0, ti:ti + 1], i=src3[:, ti, :]:
               v.reduce_sum(o, i, axis=AXX), sig=True, reads=(rdres,), writes=("stt",))
            em("scalar", lambda s, o=xnfb[:, 0:E], i=src3[:, ti, :], a=st16[:, 1, ti:ti + 1]:
               s.activation(o, i, AF.Square, accum_out=a),
               sig=True, reads=(rdres,), writes=("stt", "xnf"))
        em("vector", lambda v, o=st16[:, 2, :], i=st16[:, 0, :]: v.tensor_scalar_mul(o, i, 1.0 / E),
           sig=True, writes=("stt",))
        em("vector", lambda v, o=st16[:, 3, :], a=st16[:, 2, :], b=st16[:, 2, :]: v.tensor_mul(o, a, b),
           sig=True, writes=("stt",))
        em("vector", lambda v, o=st16[:, 1, :], i=st16[:, 1, :]: v.tensor_scalar_mul(o, i, 1.0 / E),
           sig=True, writes=("stt",))
        em("vector", lambda v, o=st16[:, 1, :], a=st16[:, 1, :], b=st16[:, 3, :]: v.tensor_sub(o, a, b),
           sig=True, writes=("stt",))
        em("vector", lambda v, o=st16[:, 1, :], i=st16[:, 1, :]: v.tensor_scalar_add(o, i, EPS),
           sig=True, writes=("stt",))
        em("scalar", lambda s, o=st16[:, 0, :], i=st16[:, 1, :]: s.activation(o, i, AF.Sqrt),
           sig=True, reads=("stt",), writes=("stt",))
        em("vector", lambda v, o=st16[:, 3, :], i=st16[:, 0, :]: v.reciprocal(o, i),
           sig=True, writes=("stt",))

    def phase1(s_i):
        dma("sync", "dIO", acc[:], d_x[s_i].rearrange("(a p) d -> p a d", p=128), writes=("acc",))
        dma("gpsimd", "dL", bob_s[:], d_bob, writes=("bob",))
        ln_stats_all(acc, "acc")
        for ti in range(NT):
            em("vector", lambda v, o=xnf[:], i=acc[:, ti, :], m=st16[:, 2, ti:ti + 1],
               r=st16[:, 3, ti:ti + 1]:
               v.tensor_scalar(o, i, m, r, op0=ALU.subtract, op1=ALU.mult),
               sig=True, reads=("stt", "acc"), writes=("xnf",))
            em("scalar", lambda s, o=u[:, ti, :], i=xnf[:]: s.copy(o, i),
               sig=True, reads=("xnf",), writes=("u",))
            em("vector", lambda v, o=acc[:, ti, :], a=acc[:, ti, :], b=bob_s[:]:
               v.tensor_add(o, a, b), sig=True, reads=("bob",), writes=("acc",))
            for g in range(NG):
                em("tensor", lambda t, o=pT[:, g * 128:(g + 1) * 128],
                   i=u[:, ti, g * 128:(g + 1) * 128], d=id_s[:]: t.transpose(o, i, d),
                   sig=(g == NG - 1), reads=("u",),
                   writes=("pT",) if g in (0, NG - 1) else ())
            em("scalar", lambda s, o=xnT[:, :, ti * 128:(ti + 1) * 128],
               i=pT[:].rearrange("p (g t) -> p g t", g=NG): s.copy(o, i),
               sig=True, reads=("pT",), writes=("xnT",))

    def rope(px, dst_ap, dstres, src_ap, srcres, rot, rotres, sl, psl):
        em("tensor", lambda t, o=pG[:], l=rot[:], r=src_ap[:, sl]:
           t.matmul(o, l, r, start=True, stop=True),
           sig=True, reads=(srcres, rotres), writes=(px + "pG",))
        em("vector", lambda v, o=tm1[:], a=src_ap[:, sl], b=cos_s[:, psl]:
           v.tensor_mul(o, a, b), sig=True, reads=(srcres, "cos"), writes=(px + "tm1",))
        em("vector", lambda v, o=pG[:], a=pG[:], b=sin_s[:, psl]:
           v.tensor_mul(o, a, b), sig=True, reads=("sin",), writes=(px + "pG",))
        em("vector", lambda v, o=dst_ap[:, sl], a=tm1[:], b=pG[:]:
           v.tensor_add(o, a, b), sig=True, reads=(px + "tm1", px + "pG"), writes=(dstres,))

    def attention(px, inT, inres, wq_d, wk_d, wv_d, prj_d, has_bias, rot, rotres, accum_u,
                  dumpfn=None):
        Kr = [Kr0, Kr1]
        Qr = Qrs
        Ob = [Ob0, Ob1]
        dma("gpsimd", "dL", cos_s[:], d_cos, writes=("cos",))
        dma("gpsimd", "dL", sin_s[:], d_sin, writes=("sin",))
        dma("gpsimd", "dL", mb_s[:], d_mb, writes=("mb",))
        if has_bias:
            dma("gpsimd", "dL", bvb_s[:], d_bvb, writes=("bvb",))
        for hg in range(HG):
            dq0 = hg * 256
            dma("gpsimd", "dL", wq_b[:], wq_d[:, dq0:dq0 + 256].rearrange("(c p) m -> p c m", p=128),
                writes=(px + "wqb",))
            dma("gpsimd", "dL", wk_b[:], wk_d[:, dq0:dq0 + 256].rearrange("(c p) m -> p c m", p=128),
                writes=(px + "wkb",))
            dma("gpsimd", "dL", wv_b[:], wv_d[:, dq0:dq0 + 256].rearrange("(c p) m -> p c m", p=128),
                writes=(px + "wvb",))
            dma("gpsimd", "dL", wo_b[:], prj_d[dq0:dq0 + 256, :].rearrange("(c p) m -> p c m", p=128),
                writes=(px + "wob",))
            for g2 in range(2):
                krres = px + f"Kr{g2}"
                cidx = hg * 2 + g2
                for j in range(4):
                    sl = slice(j * 512, (j + 1) * 512)
                    pb, pbr = (pA, px + "pA") if j % 2 == 0 else (pB, px + "pB")
                    for kc in range(NG):
                        em("tensor", lambda t, o=pb[:], l=wq_b, w=wk_b[:, kc, g2 * 128:(g2 + 1) * 128],
                           r=inT[:, kc, sl], s0=(kc == 0), s1=(kc == NG - 1):
                           t.matmul(o, w, r, start=s0, stop=s1),
                           sig=(kc == NG - 1), reads=(px + "wkb", inres),
                           writes=(pbr,) if kc in (0, NG - 1) else ())
                    if has_bias:
                        em("scalar", lambda s, o=Kr[g2][:, sl], i=pb[:], b=bk_s[:, cidx:cidx + 1]:
                           s.activation(o, i, AF.Identity, bias=b),
                           sig=True, reads=(pbr, "bkc"), writes=(krres,))
                    else:
                        em("scalar", lambda s, o=Kr[g2][:, sl], i=pb[:]: s.copy(o, i),
                           sig=True, reads=(pbr,), writes=(krres,))
                for j in range(4):
                    sl = slice(j * 512, (j + 1) * 512)
                    rope(px, Kr[g2], krres, Kr[g2], krres, rot, rotres, sl, sl)
            for ti in range(NT):
                pb, pbr = (pA, px + "pA") if ti % 2 == 0 else (pB, px + "pB")
                for kc in range(NG):
                    em("tensor", lambda t, o=pb[:, 0:256], l=inT[:, kc, ti * 128:(ti + 1) * 128],
                       r=wv_b[:, kc, :], s0=(kc == 0), s1=(kc == NG - 1):
                       t.matmul(o, l, r, start=s0, stop=s1),
                       sig=(kc == NG - 1), reads=(px + "wvb", inres),
                       writes=(pbr,) if kc in (0, NG - 1) else ())
                if has_bias:
                    em("vector", lambda v, o=Vg[:, ti, :, 0:64],
                       a=pb[:, 0:256].rearrange("p (h j) -> p h j", h=4),
                       b=bvb_s[:, dq0:dq0 + 256].rearrange("p (h j) -> p h j", h=4):
                       v.tensor_add(o, a, b), sig=True, reads=(pbr, "bvb"), writes=(px + "Vg",))
                else:
                    em("scalar", lambda s, o=Vg[:, ti, :, 0:64],
                       i=pb[:, 0:256].rearrange("p (h j) -> p h j", h=4):
                       s.copy(o, i), sig=True, reads=(pbr,), writes=(px + "Vg",))
            em("vector", lambda v, o=Vg[:, :, :, 64:65]: v.memset(o, 1.0),
               sig=True, writes=(px + "Vg",))
            if dumpfn is not None and hg == 0:
                dumpfn("iKr0", Kr0[:], [128, S])
                dumpfn("iKr1", Kr1[:], [128, S])
                dumpfn("iVg", Vg[:], [128, NT, 4, 65])

            def emit_q(qb):
                q0 = qb * 512
                qp = qb % 2
                for g2 in range(2):
                    qrres = px + f"Q{g2}{qp}"
                    cidx = hg * 2 + g2
                    for kc in range(NG):
                        em("tensor", lambda t, o=pG[:], l=wq_b[:, kc, g2 * 128:(g2 + 1) * 128],
                           r=inT[:, kc, q0:q0 + 512], s0=(kc == 0), s1=(kc == NG - 1):
                           t.matmul(o, l, r, start=s0, stop=s1),
                           sig=(kc == NG - 1), reads=(px + "wqb", inres),
                           writes=(px + "pG",) if kc in (0, NG - 1) else ())
                    if has_bias:
                        em("scalar", lambda s, o=Qr[g2][qp][:], i=pG[:], b=bq_s[:, cidx:cidx + 1]:
                           s.activation(o, i, AF.Identity, bias=b),
                           sig=True, reads=(px + "pG", "bqc"), writes=(qrres,))
                    else:
                        em("scalar", lambda s, o=Qr[g2][qp][:], i=pG[:]: s.copy(o, i),
                           sig=True, reads=(px + "pG",), writes=(qrres,))
                    rope(px, Qr[g2][qp], qrres, Qr[g2][qp], qrres, rot, rotres,
                         slice(0, 512), slice(q0, q0 + 512))

            emit_q(0)
            for qb in range(4):
                q0 = qb * 512
                qp = qb % 2
                kmax = (q0 + 512) // 128
                SBB = [pA, pB, pC, pD]
                SBr = [px + "pA", px + "pB", px + "pC", px + "pD"]
                pairs = [(h2, kb) for h2 in range(4) for kb in range(kmax)]

                def emit_av(i):
                    h2, kb = pairs[i]
                    g2 = h2 // 2
                    po, por = (pE, px + "pE") if h2 % 2 == 0 else (pF, px + "pF")
                    ex, exr = EXs[i % 4], px + f"EX{i % 4}"
                    em("tensor", lambda t, o=po[0:65, :], l=Vg[:, kb, h2, :], r=ex[:],
                       s0=(kb == 0), s1=(kb == kmax - 1): t.matmul(o, l, r, start=s0, stop=s1),
                       sig=(kb == kmax - 1), reads=(px + "Vg", exr),
                       writes=(por,) if kb in (0, kmax - 1) else ())
                    if kb == kmax - 1:
                        ho = (h2 % 2) * 64

                        def _recip(v, o=rinv[:], i_=po[64:65, :]):
                            with nc.allow_low_precision(reason="bf16 softmax normalizer"):
                                return v.reciprocal(o, i_)
                        em("vector", _recip, sig=True, reads=(por,), writes=(px + "rinv",))
                        em("tensor", lambda t, o=pG[0:64, :], l=ones_s[:, 0:64], r=rinv[:]:
                           t.matmul(o, l, r, start=True, stop=True),
                           sig=True, reads=(px + "rinv",), writes=(px + "pG",))
                        em("scalar", lambda s, o=rb_s[:], i=pG[0:64, :]: s.copy(o, i),
                           sig=True, reads=(px + "pG",), writes=(px + "rb",))
                        em("vector", lambda v, o=Ob[g2][ho:ho + 64, :], a=po[0:64, :], b=rb_s[:]:
                           v.tensor_mul(o, a, b), sig=True, reads=(por, px + "rb"),
                           writes=(px + f"Ob{g2}",))

                for i, (h2, kb) in enumerate(pairs):
                    g2, ho = h2 // 2, (h2 % 2) * 64
                    qrres = px + f"Q{g2}{qp}"
                    ri = i % 4
                    ps, psr = SBB[ri], SBr[ri]
                    ex, exr = EXs[ri], px + f"EX{ri}"
                    em("tensor", lambda t, o=ps[:], l=Kr[g2][ho:ho + 64, kb * 128:(kb + 1) * 128],
                       r=Qr[g2][qp][ho:ho + 64, :]: t.matmul(o, l, r, start=True, stop=True),
                       sig=True, reads=(px + f"Kr{g2}", qrres), writes=(psr,))
                    em("scalar", lambda s, o=ex[:], i=ps[:]: s.activation(o, i, AF.Exp, scale=0.125),
                       sig=True, reads=(psr,), writes=(exr,))
                    rel = kb * 128 - q0
                    if rel >= 0:
                        c0 = 384 - rel
                        em("vector", lambda v, o=ex[:], a=ex[:], b=mb_s[:, c0:c0 + 512]:
                           v.tensor_mul(o, a, b), sig=True, reads=("mb",), writes=(exr,))
                    if i >= 3:
                        emit_av(i - 3)
                    if i == kmax and qb + 1 < 4:
                        emit_q(qb + 1)
                for i in range(max(0, len(pairs) - 3), len(pairs)):
                    emit_av(i)

                if dumpfn is not None and hg == 0 and qb == 0:
                    dumpfn("iQr0", Qr[0][0][:], [128, 512])
                    dumpfn("iQr1", Qr[1][0][:], [128, 512])
                    dumpfn("iOb0", Ob0[:], [128, 512])
                    dumpfn("iOb1", Ob1[:], [128, 512])
                for ti in range(4):
                    tk = qb * 4 + ti
                    for eh in range(2):
                        pb, pbr = (pA, px + "pA") if eh == 0 else (pB, px + "pB")
                        for g2 in range(2):
                            em("tensor", lambda t, o=pb[:], l=Ob[g2][:, ti * 128:(ti + 1) * 128],
                               r=wo_b[:, g2, eh * 512:(eh + 1) * 512], s0=(g2 == 0), s1=(g2 == 1):
                               t.matmul(o, l, r, start=s0, stop=s1),
                               sig=(g2 == 1), reads=(px + f"Ob{g2}", px + "wob"),
                               writes=(pbr,))
                        if accum_u:
                            em("vector", lambda v, o=u[:, tk, eh * 512:(eh + 1) * 512],
                               a=u[:, tk, eh * 512:(eh + 1) * 512], b=pb[:]:
                               v.tensor_add(o, a, b), sig=True, reads=(pbr,), writes=("u",))
                            if hg == HG - 1:
                                em("vector", lambda v, o=acc[:, tk, eh * 512:(eh + 1) * 512],
                                   a=acc[:, tk, eh * 512:(eh + 1) * 512],
                                   b=u[:, tk, eh * 512:(eh + 1) * 512]:
                                   v.tensor_add(o, a, b), sig=True, reads=("u",), writes=("acc",))
                        else:
                            em("vector", lambda v, o=acc[:, tk, eh * 512:(eh + 1) * 512],
                               a=acc[:, tk, eh * 512:(eh + 1) * 512], b=pb[:]:
                               v.tensor_add(o, a, b), sig=True, reads=(pbr,), writes=("acc",))

    def ln_transpose(px, wap, bap, do_conv, outT, outres):
        ln_stats_all(u, "u")
        for ti in range(NT):
            em("vector", lambda v, o=xnf[:], i=u[:, ti, :], m=st16[:, 2, ti:ti + 1],
               r=st16[:, 3, ti:ti + 1]:
               v.tensor_scalar(o, i, m, r, op0=ALU.subtract, op1=ALU.mult),
               sig=True, reads=("stt", "u"), writes=("xnf",))
            em("scalar", lambda s, o=n1buf[:, ti, :], i=xnf[:]: s.copy(o, i),
               sig=True, reads=("xnf",), writes=(px + "n1",))
        for g in range(NG):
            for tb in range(2):
                for k in range(8):
                    ti = tb * 8 + k
                    em("tensor", lambda t, o=pT[:, k * 128:(k + 1) * 128],
                       i=n1buf[:, ti, g * 128:(g + 1) * 128], d=id_s[:]: t.transpose(o, i, d),
                       sig=(k == 7), reads=(px + "n1",),
                       writes=("pT",) if k in (0, 7) else ())
                em("vector", lambda v, o=ct[:, tb * 1024:(tb + 1) * 1024], i=pT[:],
                   w=wap[:, g:g + 1], b=bap[:, g:g + 1]:
                   v.tensor_scalar(o, i, w, b, op0=ALU.mult, op1=ALU.add),
                   sig=True, reads=("pT",), writes=(px + "ct",))
            if do_conv:
                em("vector", lambda v, o=ct2[:], i=ct[:], w=cw_s[:, g, 2:3]:
                   v.tensor_scalar_mul(o, i, w), sig=True, reads=(px + "ct",), writes=(px + "ct2",))
                em("vector", lambda v, o=ct2[:, 1:S], i0=ct[:, 0:S - 1], w=cw_s[:, g, 1:2],
                   i1=ct2[:, 1:S]: v.scalar_tensor_tensor(o, i0, w, i1, op0=ALU.mult, op1=ALU.add),
                   sig=True, reads=(px + "ct",), writes=(px + "ct2",))
                em("vector", lambda v, o=outT[:, g, 2:S], i0=ct[:, 0:S - 2], w=cw_s[:, g, 0:1],
                   i1=ct2[:, 2:S]: v.scalar_tensor_tensor(o, i0, w, i1, op0=ALU.mult, op1=ALU.add),
                   sig=True, reads=(px + "ct", px + "ct2"), writes=(outres,))
                em("scalar", lambda s, o=outT[:, g, 0:2], i=ct2[:, 0:2]: s.copy(o, i),
                   sig=True, reads=(px + "ct2",), writes=(outres,))
            else:
                em("scalar", lambda s, o=outT[:, g, :], i=ct[:]: s.copy(o, i),
                   sig=True, reads=(px + "ct",), writes=(outres,))
                for j in range(4):
                    em("tensor", lambda t, o=pG[0:NEXP, :], l=rw_s[:, g, :],
                       r=ct[:, j * 512:(j + 1) * 512]: t.matmul(o, l, r, start=True, stop=True),
                       sig=True, reads=(px + "ct",), writes=(px + "pG",))
                    if g == 0:
                        em("vector", lambda v, o=logits[:, j * 512:(j + 1) * 512], i=pG[0:NEXP, :]:
                           v.tensor_copy(o, i), sig=True, reads=(px + "pG",), writes=("logits",))
                    else:
                        em("vector", lambda v, o=logits[:, j * 512:(j + 1) * 512],
                           a=logits[:, j * 512:(j + 1) * 512], i=pG[0:NEXP, :]:
                           v.tensor_add(o, a, i), sig=True, reads=(px + "pG",), writes=("logits",))

    def build_masks():
        for j in range(3):
            dma("sync", "dIO", lrow[:, j, :], logits[j + 1:j + 2, :],
                reads=("logits",), writes=("lrow",))
        l_ = [logits[0:1, :], lrow[:, 0, :], lrow[:, 1, :], lrow[:, 2, :]]
        specs = [
            (0, [(0, 1, ALU.is_ge), (0, 2, ALU.is_ge), (0, 3, ALU.is_ge)]),
            (1, [(1, 0, ALU.is_gt), (1, 2, ALU.is_ge), (1, 3, ALU.is_ge)]),
            (2, [(2, 0, ALU.is_gt), (2, 1, ALU.is_gt), (2, 3, ALU.is_ge)]),
            (3, [(3, 0, ALU.is_gt), (3, 1, ALU.is_gt), (3, 2, ALU.is_gt)]),
        ]
        for e, cs in specs:
            for k, (a, b, op) in enumerate(cs):
                em("vector", lambda v, o=mtmp[:, k, :], aa=l_[a], bb=l_[b], p=op:
                   v.tensor_tensor(o, aa, bb, p), sig=True,
                   reads=("logits", "lrow"), writes=("mtmp",))
            em("vector", lambda v, o=mtmp[:, 0, :], a=mtmp[:, 0, :], b=mtmp[:, 1, :]:
               v.tensor_mul(o, a, b), sig=True, writes=("mtmp",))
            em("vector", lambda v, o=mrow[:, e, :], a=mtmp[:, 0, :], b=mtmp[:, 2, :]:
               v.tensor_mul(o, a, b), sig=True, reads=("mtmp",), writes=("mrow",))

    def moe(px):
        for th in range(2):
            t0 = th * 1024
            for e in range(NEXP):
                for j in range(2):
                    em("tensor", lambda t, o=pG[:], l=ones_s[:],
                       r=mrow[:, e, t0 + j * 512:t0 + (j + 1) * 512]:
                       t.matmul(o, l, r, start=True, stop=True),
                       sig=True, reads=("mrow",), writes=(px + "pG",))
                    em("scalar", lambda s, o=maskb[:, j * 512:(j + 1) * 512], i=pG[:]:
                       s.copy(o, i), sig=True, reads=(px + "pG",), writes=(px + "mkb",))
                for zg in range(4):
                    c0 = zg * 8
                    for q in range(4):
                        rr = (zg * 4 + q) % 2
                        gsl = slice((c0 + q * 2) * 128, (c0 + q * 2 + 2) * 128)
                        vsl = slice(DFF + (c0 + q * 2) * 128, DFF + (c0 + q * 2 + 2) * 128)
                        dma("gpsimd", "dL", wg_b[rr][:],
                            d_ewv[e][:, gsl].rearrange("(c p) m -> p c m", p=128),
                            writes=(px + f"wg{rr}",))
                        dma("gpsimd", "dL", wv2_b[rr][:],
                            d_ewv[e][:, vsl].rearrange("(c p) m -> p c m", p=128),
                            writes=(px + f"wv2{rr}",))
                        for cc in range(2):
                            c8 = q * 2 + cc
                            for nb in range(2):
                                par = (c8 * 2 + nb) % 2
                                pg_, pgr = (pA, px + "pA") if par == 0 else (pC, px + "pC")
                                pv_, pvr = (pB, px + "pB") if par == 0 else (pD, px + "pD")
                                for kc in range(NG):
                                    em("tensor", lambda t, o=pg_[:],
                                       l=wg_b[rr][:, kc, cc * 128:(cc + 1) * 128],
                                       r=h2T[:, kc, t0 + nb * 512:t0 + (nb + 1) * 512],
                                       s0=(kc == 0), s1=(kc == NG - 1):
                                       t.matmul(o, l, r, start=s0, stop=s1),
                                       sig=(kc == NG - 1), reads=(px + f"wg{rr}", "h2T"),
                                       writes=(pgr,) if kc in (0, NG - 1) else ())
                                for kc in range(NG):
                                    em("tensor", lambda t, o=pv_[:],
                                       l=wv2_b[rr][:, kc, cc * 128:(cc + 1) * 128],
                                       r=h2T[:, kc, t0 + nb * 512:t0 + (nb + 1) * 512],
                                       s0=(kc == 0), s1=(kc == NG - 1):
                                       t.matmul(o, l, r, start=s0, stop=s1),
                                       sig=(kc == NG - 1), reads=(px + f"wv2{rr}", "h2T"),
                                       writes=(pvr,) if kc in (0, NG - 1) else ())
                                em("scalar", lambda s, o=sgs[:], i=pg_[:]: s.activation(o, i, AF.Silu),
                                   sig=True, reads=(pgr,), writes=(px + "sgs",))
                                em("vector", lambda v, o=pv_[:], a=pv_[:],
                                   b=maskb[:, nb * 512:(nb + 1) * 512]:
                                   v.tensor_mul(o, a, b), sig=True,
                                   reads=(px + "mkb",), writes=(pvr,))
                                em("vector", lambda v, o=z_s[:, c8, nb * 512:(nb + 1) * 512],
                                   a=sgs[:], b=pv_[:]: v.tensor_mul(o, a, b),
                                   sig=True, reads=(px + "sgs", pvr), writes=(px + "z",))
                    for ph in range(2):
                        rsl = slice((c0 + ph * 4) * 128, (c0 + ph * 4 + 4) * 128)
                        dma("gpsimd", "dL", P_b[ph][:],
                            d_epr[e][rsl, :].rearrange("(c p) m -> p c m", p=128),
                            writes=(px + f"P{ph}",))
                    for ti in range(8):
                        tk = th * 8 + ti
                        for eh in range(2):
                            po, por = (pE, px + "pE") if (ti * 2 + eh) % 2 == 0 else (pF, px + "pF")
                            for c8 in range(8):
                                rr2 = c8 // 4
                                em("tensor", lambda t, o=po[:], l=z_s[:, c8, ti * 128:(ti + 1) * 128],
                                   r=P_b[rr2][:, c8 % 4, eh * 512:(eh + 1) * 512],
                                   s0=(c8 == 0), s1=(c8 == 7): t.matmul(o, l, r, start=s0, stop=s1),
                                   sig=(c8 == 7), reads=(px + "z", px + f"P{rr2}"),
                                   writes=(por,) if c8 in (0, 7) else ())
                            em("vector", lambda v, o=acc[:, tk, eh * 512:(eh + 1) * 512],
                               a=acc[:, tk, eh * 512:(eh + 1) * 512], b=po[:]:
                               v.tensor_add(o, a, b), sig=True, reads=(por,), writes=("acc",))

    dbg_tensors = {}

    def dump(name, sb_ap, shape):
        if not dbg:
            return
        if name not in dbg_tensors:
            dbg_tensors[name] = nc.dram_tensor("dbg_" + name, shape, sb_ap.tensor.dtype,
                                               kind="ExternalOutput").ap()
        barrier()
        dma("sync", "dIO", dbg_tensors[name], sb_ap)
        barrier()

    final_toks = []
    for s_i in range(n_seq):
        phase1(s_i)
        barrier()
        if s_i == 0:
            dump("u1", u[:], [128, NT, E])
            dump("acc1", acc[:], [128, NT, E])
            dump("xnT", xnT[:], [128, NG, S])
        attention(f"A{s_i}", xnT, "xnT", d_wqT, d_wkT, d_wvT, d_woT, True, r1_s, "r1", False)
        barrier()
        if s_i == 0:
            dump("acc2", acc[:], [128, NT, E])
        ln_transpose(f"L{s_i}", ln1w_s, ln1b_s, True, h1c, "h1c")
        barrier()
        if s_i == 0:
            dump("h1c", h1c[:], [128, NG, S])
        attention(f"B{s_i}", h1c, "h1c", d_aw[:, 0:E], d_aw[:, E:2 * E], d_aw[:, 2 * E:3 * E],
                  d_pw, False, r2_s, "r2", True,
                  dumpfn=(dump if (dbg and s_i == 0) else None))
        barrier()
        if s_i == 0:
            dump("u2", u[:], [128, NT, E])
            dump("acc3", acc[:], [128, NT, E])
        ln_transpose(f"M{s_i}", ln2w_s, ln2b_s, False, h2T, "h2T")
        barrier()
        if s_i == 0:
            dump("h2T", h2T[:], [128, NG, S])
            dump("logits", logits[:], [NEXP, S])
        build_masks()
        barrier()
        if s_i == 0:
            dump("mrow", mrow[:], [1, NEXP, S])
        moe(f"X{s_i}")
        tok = dma("sync", "dIO", d_y[s_i].rearrange("(a p) d -> p a d", p=128), acc[:],
                  reads=("acc",))
        final_toks.append(tok)
        barrier()
        for n in ("xnT", "h1c", "h2T", "logits", "lrow", "mtmp", "mrow", "cos", "sin",
                  "mb", "bvb", "bob"):
            R.pop(n, None)

    ops["sync"].append(([("dIO", final_toks[-1][1])], lambda sy: sy.nop(), None))

    with contextlib.ExitStack() as stk:
        sems = {nm: stk.enter_context(nc.semaphore(nm)) for nm in SEMS}
        with nc.Block() as blk:
            def mk(en):
                def body(e):
                    for waits, fn, tok in ops[en]:
                        for (s_, v_) in waits:
                            e.wait_ge(sems[s_], v_)
                        inst = fn(e)
                        if tok is not None:
                            inst.then_inc(sems[tok[0]], tok[2])
                return body
            blk.tensor(mk("tensor"))
            blk.vector(mk("vector"))
            blk.scalar(mk("scalar"))
            blk.sync(mk("sync"))
            blk.gpsimd(mk("gpsimd"))
    return nc


def host_prep(inputs):
    f32 = np.float32

    def c(a):
        return np.ascontiguousarray(a, dtype=f32)

    wq = np.asarray(inputs["L0_attn_wq"])
    wk = np.asarray(inputs["L0_attn_wk"])
    wv = np.asarray(inputs["L0_attn_wv"])
    wo = np.asarray(inputs["L0_attn_wo"])
    bq = np.asarray(inputs["L0_attn_bq"])
    bk = np.asarray(inputs["L0_attn_bk"])
    bv = np.asarray(inputs["L0_attn_bv"])
    bo = np.asarray(inputs["L0_attn_bo"])
    aw = np.asarray(inputs["L0_ffn_attn_w"])
    pw = np.asarray(inputs["L0_ffn_proj_w"])
    cwf = np.asarray(inputs["L0_ffn_conv_w"])
    ln1w = np.asarray(inputs["L0_ffn_ln1_w"])
    ln1b = np.asarray(inputs["L0_ffn_ln1_b"])
    ln2w = np.asarray(inputs["L0_ffn_ln2_w"])
    ln2b = np.asarray(inputs["L0_ffn_ln2_b"])
    rw = np.asarray(inputs["L0_ffn_router_w"])
    ewv = np.asarray(inputs["L0_ffn_expert_wv"])
    epr = np.asarray(inputs["L0_ffn_expert_proj"])

    def pp(b):
        return c(np.asarray(b).reshape(NG, 128).T)

    inv = (10000.0 ** (-np.arange(0, 64, 2, dtype=np.float64) / 64.0))
    t = np.arange(S, dtype=np.float64)
    freqs = t[:, None] * inv[None, :]
    emb = np.concatenate([freqs, freqs], axis=-1)
    c64 = np.cos(emb).T
    s64 = np.sin(emb).T
    costab = c(np.tile(c64, (2, 1)))
    sintab = c(np.tile(s64, (2, 1)))
    p_ = np.arange(128)[:, None]
    cidx = np.arange(896)[None, :]
    maskbin = c((p_ <= cidx - 384).astype(f32))
    R1 = np.zeros((64, 64), f32)
    for i in range(32):
        R1[2 * i, 2 * i + 1] = -1.0
        R1[2 * i + 1, 2 * i] = 1.0
    R2 = np.zeros((64, 64), f32)
    for i in range(32):
        R2[i, i + 32] = -1.0
        R2[i + 32, i] = 1.0

    def blk2(Rm):
        out = np.zeros((128, 128), f32)
        out[0:64, 0:64] = Rm
        out[64:128, 64:128] = Rm
        return c(out.T)

    d = {
        "wqT": c(wq.T), "wkT": c(wk.T), "wvT": c(wv.T), "woT": c(wo.T),
        "bq": pp(bq), "bk": pp(bk),
        "bvb": c(np.broadcast_to(bv, (128, E))), "bob": c(np.broadcast_to(bo, (128, E))),
        "aw": c(aw), "pw": c(pw),
        "cw": c(cwf[:, 0, :].reshape(NG, 128, 3).transpose(1, 0, 2)),
        "ln1w": pp(ln1w), "ln1b": pp(ln1b), "ln2w": pp(ln2w), "ln2b": pp(ln2b),
        "rw": c(rw.reshape(NG, 128, NEXP).transpose(1, 0, 2)),
        "ewv": c(ewv), "epr": c(epr),
        "costab": costab, "sintab": sintab, "maskbin": maskbin,
        "r1t": blk2(R1), "r2t": blk2(R2),
        "identin": c(np.eye(128, dtype=f32)),
        "onesin": c(np.ones((1, 128), f32)),
    }
    return d


_CACHED = {}


def kernel(**inputs):
    x = np.ascontiguousarray(np.asarray(inputs["L0_attn_x"], dtype=np.float32))
    B = x.shape[0]
    n_cores = 8
    per = B // n_cores
    shared = host_prep(inputs)
    key = per
    if key not in _CACHED:
        _CACHED[key] = build_program(per)
    nc = _CACHED[key]
    in_maps = []
    for cix in range(n_cores):
        m = dict(shared)
        m["x"] = np.ascontiguousarray(x[cix * per:(cix + 1) * per])
        in_maps.append(m)
    res = run_bass_kernel_spmd(nc, in_maps, core_ids=list(range(n_cores)))
    out = np.concatenate([np.asarray(r["y"]) for r in res.results], axis=0)
    return out.astype(np.float32)
